# version 21
# baseline (speedup 1.0000x reference)
import numpy as np
import ml_dtypes
import concourse.bass as bass
import concourse.mybir as mybir
from concourse.bass_utils import run_bass_kernel_spmd

F32 = mybir.dt.float32
BF16 = mybir.dt.bfloat16
AX = mybir.AxisListType.X
AF = mybir.ActivationFunctionType
OP = mybir.AluOpType

NCORES = 8
NSEQ = 4
S = 2048
D = 1024
FF = 4096
HD = 64
TILE = 512
NT_SEQ = S // TILE
NTILES = NSEQ * NT_SEQ
NDC = D // 128
NFC = FF // 128
NW = FF * D

ENGS = ("sync", "gpsimd", "vector", "scalar", "tensor")
ESEM = {"tensor": "st", "vector": "sv", "scalar": "ss"}


class Build:

    def __init__(self):
        self.ops = {e: [] for e in ENGS}
        self.cnt = {}
        self.hw = {e: {} for e in ENGS}
        self.lw = {}
        self.lr = {}
        self.pending = {e: [] for e in ENGS}

    def n(self, sem):
        return self.cnt.get(sem, 0)

    def _wait(self, eng, sem, val):
        if val <= self.hw[eng].get(sem, 0):
            return
        assert val <= self.n(sem), f"wait-ahead: {eng} waits {sem}>={val} but only {self.n(sem)} emitted"
        self.hw[eng][sem] = val
        self.ops[eng].append(("w", sem, val))

    def raw_wait(self, eng, sem, val):
        self._wait(eng, sem, val)

    def op(self, eng, fn, reads=(), writes=(), inc=None, by=1, extra=()):
        waits = {}

        def add(ent):
            if ent["eng"] == eng:
                return
            assert ent["val"] is not None, f"pending guard on {ent} needed by {eng}"
            sem, val = ent["sem"], ent["val"]
            waits[sem] = max(waits.get(sem, 0), val)

        for b in reads:
            for e in self.lw.get(b, {}).values():
                add(e)
        for b in writes:
            for e in self.lw.get(b, {}).values():
                add(e)
            for e2 in self.lr.get(b, {}).values():
                add(e2)
        for sem, val in extra:
            waits[sem] = max(waits.get(sem, 0), val)
        for sem in sorted(waits):
            self._wait(eng, sem, waits[sem])

        if inc is not None:
            self.cnt[inc] = self.n(inc) + by
        myc = self.n(inc) if inc is not None else None
        self.ops[eng].append(("i", fn, inc, by))

        if inc is not None:
            ent = {"eng": eng, "sem": inc, "val": myc}
            if inc == ESEM.get(eng):
                for p in self.pending[eng]:
                    p["sem"], p["val"] = inc, myc
                self.pending[eng] = []
        else:
            ent = {"eng": eng, "sem": ESEM[eng], "val": None}
            self.pending[eng].append(ent)
        for b in writes:
            self.lw[b] = {eng: ent}
            self.lr[b] = {}
        for b in reads:
            self.lr.setdefault(b, {})[eng] = ent
        return myc


def _build(taps=None, dev=False, dev_tiles=None, dump_tile=0, dev_phases="all", prof=False):
    taps = taps or []
    nc = bass.Bass("TRN2", target_bir_lowering=False, detect_race_conditions=False)

    xT = nc.dram_tensor("xT", [NSEQ, D, S], F32, kind="ExternalInput")
    w1blk = nc.dram_tensor("w1blk", [NFC, 128, NDC * 128], F32, kind="ExternalInput")
    w2blk = nc.dram_tensor("w2blk", [NFC, 128, D], F32, kind="ExternalInput")
    wq_d = nc.dram_tensor("wq", [D, HD], F32, kind="ExternalInput")
    wk_d = nc.dram_tensor("wk", [D, HD], F32, kind="ExternalInput")
    wv_d = nc.dram_tensor("wv", [D, HD], F32, kind="ExternalInput")
    wo_d = nc.dram_tensor("wo", [HD, D], F32, kind="ExternalInput")
    b1_d = nc.dram_tensor("b1r", [128, NFC], F32, kind="ExternalInput")
    b2_d = nc.dram_tensor("b2r", [128, NDC], F32, kind="ExternalInput")
    bo_d = nc.dram_tensor("bor", [128, NDC], F32, kind="ExternalInput")
    bq_d = nc.dram_tensor("bqr", [HD, 1], F32, kind="ExternalInput")
    bk_d = nc.dram_tensor("bkr", [HD, 1], F32, kind="ExternalInput")
    bv_d = nc.dram_tensor("bvr", [HD, 1], F32, kind="ExternalInput")
    id_d = nc.dram_tensor("identf", [128, 128], F32, kind="ExternalInput")
    mk_d = nc.dram_tensor("maskt", [128, 512], F32, kind="ExternalInput")
    scr_kind = "ExternalOutput" if dev else "Internal"
    scr1 = nc.dram_tensor("scr1", [NFC, 128, NDC * 128], BF16, kind=scr_kind)
    scr2 = nc.dram_tensor("scr2", [NFC, 128, D], BF16, kind=scr_kind)
    yT = nc.dram_tensor("yT", [NSEQ, D, S], F32, kind="ExternalOutput")
    tap_d = {}
    for (g, ph) in taps:
        tap_d[(g, ph)] = nc.dram_tensor(f"tap_{g}_{ph}", [128, NDC * TILE], F32,
                                        kind="ExternalOutput")

    A = nc.alloc_sbuf_tensor
    xb = [A(f"xb{k}", [128, NDC * TILE], F32) for k in range(2)]
    xnh = A("xnh", [128, NDC * TILE], BF16)
    xnl = A("xnl", [128, NDC * TILE], BF16)
    hh = A("hh", [128, 8 * TILE], BF16)
    hl = A("hl", [128, 8 * TILE], BF16)
    ysb = A("ysb", [128, NDC * TILE], F32)
    w1s = [A(f"w1s{k}", [128, 4096], BF16) for k in range(2)]
    w2s = [A(f"w2s{k}", [128, 8192], BF16) for k in range(2)]
    wpro = A("wpro", [128, 2048], F32)
    stg = A("stg", [128, 4096], BF16)
    gbuf = [A(f"g{k}", [128, TILE], F32) for k in range(2)]
    sqb = [A(f"sq{k}", [128, TILE], F32) for k in range(2)]
    mean = A("mean", [128, TILE], F32)
    rstd = A("rstd", [128, TILE], F32)
    t1 = A("t1", [128, TILE], F32)
    tsub = A("tsub", [128, TILE], F32)
    xnf = [A(f"xnf{k}", [128, TILE], F32) for k in range(2)]
    srow = A("srow", [128, S], F32)
    ptb = [A(f"ptb{k}", [128, 128], F32) for k in range(2)]
    qtile = A("qtile", [HD, TILE], F32)
    ztile = A("ztile", [HD, TILE], F32)
    vtmp = A("vtmp", [HD, TILE], F32)
    kcache = A("kcache", [HD, S], F32)
    vtc = A("vtc", [128, (S // 128) * HD], F32)
    swq = A("swq", [128, NDC * HD], F32)
    swk = A("swk", [128, NDC * HD], F32)
    swv = A("swv", [128, NDC * HD], F32)
    swo = A("swo", [HD, D], F32)
    b1s = A("b1s", [128, NFC], F32)
    b2s = A("b2s", [128, NDC], F32)
    bos = A("bos", [128, NDC], F32)
    bqs = A("bqs", [HD, 1], F32)
    bks = A("bks", [HD, 1], F32)
    bvs = A("bvs", [HD, 1], F32)
    identsb = A("identsb", [128, 128], F32)
    masks = A("masks", [128, 512], F32)
    ones = A("ones", [128, 128], F32)
    onesb = A("onesb", [128, 128], BF16)
    junk = A("junk", [128, 512], BF16)
    epss = A("epss", [128, 1], F32)
    racc = A("racc", [128, 64], F32)
    racc2 = A("racc2", [128, 2], F32)
    th = {w: A(f"th_{w}", [128, 1], F32) for w in ("1p", "1n", "2p", "2n")}
    nmax = [A(f"nmax{k}", [128, 1], F32) for k in range(2)]
    rsum = [A(f"rsum{k}", [128, 1], F32) for k in range(2)]
    rrec = [A(f"rrec{k}", [128, 1], F32) for k in range(2)]

    pb = [nc.alloc_psum_tensor(f"pb{i}", [128, 512], F32) for i in range(8)]

    sems = {k: nc.alloc_semaphore(name=k)
            for k in ("sl0", "sl1", "sl2", "sl3", "sc0", "sc1", "sc2", "sc3",
                      "sx", "sy", "st", "sv", "ss")}

    B = Build()
    _pool_idx = {"sl": 0, "sc": 0}

    def dma(eng, fn, reads=(), writes=(), pool="sl", extra=()):
        if pool in _pool_idx:
            names = {"sl": ("sl0", "sl1", "sl2", "sl3"),
                     "sc": ("sc0", "sc1", "sc2", "sc3")}[pool]
            sem = names[_pool_idx[pool] % len(names)]
            _pool_idx[pool] += 1
        else:
            sem = pool
        gate = [(sem, B.n(sem))] + list(extra)
        B.op(eng, fn, reads=reads, writes=writes, inc=sem, by=16, extra=gate)
        return (sem, B.n(sem))

    def xsl(buf, c, lo=0, hi=TILE):
        return buf[:, c * TILE + lo: c * TILE + hi]

    _dumped = set()

    def dev_dump(name, ap, reads, shape, dtype=F32):
        if not dev or name in _dumped:
            return
        _dumped.add(name)
        t = nc.dram_tensor(f"dev_{name}", list(shape), dtype, kind="ExternalOutput")
        dma("gpsimd", lambda e, d=t.ap(), s_=ap: e.dma_start(out=d, in_=s_),
            reads=reads, pool="sy")

    xsrc = xT.ap().rearrange("s (c p) t -> s p c t", p=128)
    ydst = yT.ap().rearrange("s (c p) t -> s p c t", p=128)

    def emit_xload(g):
        s, i = divmod(g, NT_SEQ)
        t0 = i * TILE
        buf = xb[g % 2]
        bn = f"xb{g % 2}"
        src = xsrc[s, :, :, t0:t0 + TILE]
        dst = buf[:].rearrange("p (c t) -> p c t", c=NDC)
        dma("gpsimd", lambda e, d=dst, s_=src: e.dma_start(out=d, in_=s_),
            writes=(bn,), pool="sx")

    emit_xload(0)

    for dst, src, name in (
        (identsb[:], id_d.ap(), "identsb"),
        (masks[:], mk_d.ap(), "masks"),
        (swq[:].rearrange("p (c h) -> p c h", c=NDC),
         wq_d.ap().rearrange("(c p) h -> p c h", p=128), "swq"),
        (swk[:].rearrange("p (c h) -> p c h", c=NDC),
         wk_d.ap().rearrange("(c p) h -> p c h", p=128), "swk"),
        (swv[:].rearrange("p (c h) -> p c h", c=NDC),
         wv_d.ap().rearrange("(c p) h -> p c h", p=128), "swv"),
        (swo[:], wo_d.ap(), "swo"),
        (b1s[:], b1_d.ap(), "b1s"),
        (b2s[:], b2_d.ap(), "b2s"),
        (bos[:], bo_d.ap(), "bos"),
        (bqs[:], bq_d.ap(), "bqs"),
        (bks[:], bk_d.ap(), "bks"),
        (bvs[:], bv_d.ap(), "bvs"),
    ):
        dma("sync", lambda e, d=dst, s_=src: e.dma_start(out=d, in_=s_),
            writes=(name,), pool="sl")

    B.op("vector", lambda e: e.memset(ones[:], 1.0), writes=("ones",), inc="sv")
    B.op("vector", lambda e: e.memset(onesb[:], 1.0), writes=("onesb",), inc="sv")
    B.op("vector", lambda e: e.memset(junk[:], 0.0), writes=("junk",), inc="sv")
    B.op("vector", lambda e: e.memset(epss[:], 1e-5), writes=("epss",), inc="sv")

    _ts = {"i": 0}

    def tslot():
        k = _ts["i"] % 6
        _ts["i"] += 1
        if k < 2:
            return wpro[:, k * 1024:(k + 1) * 1024], f"tslot{k}"
        kk = k - 2
        return xb[1][:, kk * 1024:(kk + 1) * 1024], f"tslot{k}"

    _pairs = [(wpro[:, 0:2048], "tslot0", "tslot1"),
              (xb[1][:, 0:2048], "tslot2", "tslot3"),
              (xb[1][:, 2048:4096], "tslot4", "tslot5")]

    def pass_a(blkdram, col0):
        for k in range(NFC // 2):
            dst, na, nb = _pairs[k % 3]
            src = blkdram.ap()[2 * k:2 * k + 2].rearrange("b p q -> p b q")
            dma("sync", lambda e, d=dst[:].rearrange("p (b q) -> p b q", b=2)
                    if False else dst.rearrange("p (b q) -> p b q", b=2), s_=src:
                    e.dma_start(out=d, in_=s_),
                writes=(na, nb), pool="sl")
            B.op("vector",
                 lambda e, o=racc[:, col0 + k:col0 + k + 1], i=dst:
                     e.reduce_sum(out=o, in_=i, axis=AX, apply_absolute_value=True),
                 reads=(na, nb), writes=("racc",), inc="sv")

    dev_dump("th1p", th["1p"][:], ("th1p",), (128, 1))
    dev_dump("th2p", th["2p"][:], ("th2p",), (128, 1))
    dev_dump("racc2", racc2[:], ("racc2",), (128, 2))

    _tb = {"q": 0}

    def tern_block(blkdram, scr, wname, b, scr_done):
        q = _tb["q"] % 4
        _tb["q"] += 1
        wsrc, hn = tslot()
        dma("sync", lambda e, d=wsrc, s_=blkdram.ap()[b]: e.dma_start(out=d, in_=s_),
            writes=(hn,), pool="sl")
        ge = srow[:, 0:1024]
        le = srow[:, 1024:2048]
        B.op("vector",
             lambda e, o=ge, i=wsrc, t=th[wname + "p"][:]:
                 e.tensor_scalar(out=o, in0=i, scalar1=t, scalar2=None, op0=OP.is_ge),
             reads=(hn, f"th{wname}p"), writes=("srowA",), inc="sv")
        B.op("vector",
             lambda e, o=le, i=wsrc, t=th[wname + "n"][:]:
                 e.tensor_scalar(out=o, in0=i, scalar1=t, scalar2=None, op0=OP.is_le),
             reads=(hn, f"th{wname}n"), writes=("srowB",), inc="sv")
        stq = stg[:, q * 1024:(q + 1) * 1024]
        qn = f"stage{q}"
        B.op("vector",
             lambda e, o=stq, a=ge, b_=le:
                 e.tensor_tensor(out=o, in0=a, in1=b_, op=OP.subtract),
             reads=("srowA", "srowB"), writes=(qn,), inc="sv")
        if q in (1, 3):
            ent = dma("sync",
                      lambda e, d=scr.ap()[b - 1:b + 1],
                             s_=stg[:, (q - 1) * 1024:(q + 1) * 1024]
                                 .rearrange("p (b_ j) -> p b_ j", b_=2):
                          e.dma_start(out=d.rearrange("b_ p j -> p b_ j"), in_=s_),
                      reads=(f"stage{q - 1}", qn), pool="sc")
            scr_done.append(ent)
            scr_done.append(ent)


    def warm(n, banks=(0, 1)):
        for d_ in range(n):
            bank = banks[d_ % 2]
            B.op("tensor",
                 lambda e, o=pb[bank][:]: e.matmul(o, onesb[:], junk[:],
                                                   start=True, stop=True),
                 reads=("onesb", "junk"), writes=(f"pb{bank}",),
                 inc="st" if d_ >= n - 2 else None)

    def stats_phase(g):
        bn = f"xb{g % 2}"
        buf = xb[g % 2]
        for c in range(NDC):
            k = c % 2
            B.op("vector",
                 lambda e, o=sqb[k][:], i=xsl(buf, c):
                     e.tensor_tensor(out=o, in0=i, in1=i, op=OP.mult),
                 reads=(bn,), writes=(f"sq{k}",), inc="sv")
            B.op("tensor",
                 lambda e, i=xsl(buf, c), st=(c == 0), sp=(c == NDC - 1):
                     e.matmul(pb[6][:], ones[:], i, start=st, stop=sp),
                 reads=(bn, "ones"), writes=("pb6",),
                 inc="st" if c == NDC - 1 else None)
            B.op("tensor",
                 lambda e, i=sqb[k][:], st=(c == 0), sp=(c == NDC - 1):
                     e.matmul(pb[7][:], ones[:], i, start=st, stop=sp),
                 reads=(f"sq{k}", "ones"), writes=("pb7",), inc="st")
        B.op("scalar", lambda e: e.activation(out=mean[:], in_=pb[6][:], func=AF.Copy,
                                              scale=1.0 / D),
             reads=("pb6",), writes=("mean",), inc="ss")
        B.op("scalar", lambda e: e.activation(out=t1[:], in_=pb[6][:], func=AF.Square,
                                              scale=1.0 / D),
             reads=("pb6",), writes=("t1",), inc="ss")
        B.op("vector", lambda e: e.scalar_tensor_tensor(out=tsub[:], in0=pb[7][:],
                                                        scalar=1.0 / D, in1=t1[:],
                                                        op0=OP.mult, op1=OP.subtract),
             reads=("pb7", "t1"), writes=("tsub",), inc="sv")
        B.op("scalar", lambda e: e.activation(out=t1[:], in_=tsub[:], func=AF.Sqrt,
                                              bias=epss[:]),
             reads=("tsub", "epss"), writes=("t1",), inc="ss")
        B.op("vector", lambda e: e.reciprocal(out=rstd[:], in_=t1[:]),
             reads=("t1",), writes=("rstd",), inc="sv")
        warm(20)

    def xn_units(g):
        bn = f"xb{g % 2}"
        buf = xb[g % 2]
        for c in range(NDC):
            k = c % 2
            B.op("vector",
                 lambda e, o=tsub[:], i=xsl(buf, c):
                     e.tensor_tensor(out=o, in0=i, in1=mean[:], op=OP.subtract),
                 reads=(bn, "mean"), writes=("tsub",), inc="sv")
            B.op("vector",
                 lambda e, o=xnf[k][:]:
                     e.tensor_tensor(out=o, in0=tsub[:], in1=rstd[:], op=OP.mult),
                 reads=("tsub", "rstd"), writes=(f"xnf{k}",), inc="sv")
            B.op("scalar",
                 lambda e, o=xsl(xnh, c), i=xnf[k][:]:
                     e.activation(out=o, in_=i, func=AF.Copy),
                 reads=(f"xnf{k}",), writes=(f"xnh{c}",), inc="ss")
            B.op("vector",
                 lambda e, o=xsl(xnl, c), a=xnf[k][:], b_=xsl(xnh, c):
                     e.tensor_tensor(out=o, in0=a, in1=b_, op=OP.subtract),
                 reads=(f"xnf{k}", f"xnh{c}"), writes=(f"xnl{c}",), inc="sv")
            yield

    def xn_phase(g):
        for _ in xn_units(g):
            pass

    w1_grp_ready = {}
    w2_grp_ready = {}

    def stream_w1(gidx, buf):
        src = scr1.ap()[4 * gidx:4 * gidx + 4].rearrange("b p q -> p b q")
        dst = w1s[buf][:].rearrange("p (b q) -> p b q", b=4)
        w1_grp_ready[gidx] = dma(
            "sync", lambda e, d=dst, s_=src: e.dma_start(out=d, in_=s_),
            writes=(f"w1s{buf}",), pool="sl", extra=[scr1_done[4 * gidx + 3]])

    def stream_w2(gidx, buf):
        src = scr2.ap()[8 * gidx:8 * gidx + 8].rearrange("f p q -> p f q")
        dst = w2s[buf][:].rearrange("p (f q) -> p f q", f=8)
        w2_grp_ready[gidx] = dma(
            "sync", lambda e, d=dst, s_=src: e.dma_start(out=d, in_=s_),
            writes=(f"w2s{buf}",), pool="sl", extra=[scr2_done[8 * gidx + 7]])

    def ffn_units(g, tern_hook=None):
        bn = f"xb{g % 2}"
        buf = xb[g % 2]
        stream_w1(0, 0)
        stream_w1(1, 1)
        stream_w2(0, 0)
        for G in range(4):
            if tern_hook is not None:
                tern_hook(G)
            for bloc in range(8):
                b = 8 * G + bloc
                wbuf = (b // 4) % 2
                bank = b % 2
                for c in range(NDC):
                    for p, src in ((0, xnh), (1, xnl)):
                        first = (c == 0 and p == 0)
                        last = (c == NDC - 1 and p == 1)
                        lhs = w1s[wbuf][:, (b % 4) * 1024 + c * 128:
                                        (b % 4) * 1024 + (c + 1) * 128]
                        B.op("tensor",
                             lambda e, o=pb[bank][:], l=lhs, r=xsl(src, c),
                                    st=first, sp=last:
                                 e.matmul(o, l, r, start=st, stop=sp),
                             reads=(f"w1s{wbuf}", f"xnh{c}", f"xnl{c}"),
                             writes=(f"pb{bank}",),
                             inc="st" if last else None,
                             extra=[w1_grp_ready[b // 4]])
                if b % 4 == 3 and (b // 4) + 2 <= 7:
                    stream_w1((b // 4) + 2, ((b // 4) + 2) % 2)
                gk = b % 2
                B.op("scalar",
                     lambda e, o=gbuf[gk][:], i=pb[bank][:], bia=b1s[:, b:b + 1]:
                         e.activation(out=o, in_=i, func=AF.Gelu, bias=bia),
                     reads=(f"pb{bank}", "b1s"), writes=(f"g{gk}",), inc="ss")
                B.op("scalar",
                     lambda e, o=hh[:, bloc * TILE:(bloc + 1) * TILE], i=gbuf[gk][:]:
                         e.activation(out=o, in_=i, func=AF.Copy),
                     reads=(f"g{gk}",), writes=(f"hh{bloc}",), inc="ss")
                B.op("vector",
                     lambda e, o=hl[:, bloc * TILE:(bloc + 1) * TILE], a=gbuf[gk][:],
                            b_=hh[:, bloc * TILE:(bloc + 1) * TILE]:
                         e.tensor_tensor(out=o, in0=a, in1=b_, op=OP.subtract),
                     reads=(f"g{gk}", f"hh{bloc}"), writes=(f"hl{bloc}",), inc="sv")
                yield
            if G + 1 <= 3:
                stream_w2(G + 1, (G + 1) % 2)
            w2b = G % 2
            for Dv in range(NDC):
                bank = 2 + Dv % 2
                for fl in range(8):
                    for p, src in ((0, hh), (1, hl)):
                        first = (fl == 0 and p == 0)
                        last = (fl == 7 and p == 1)
                        lhs = w2s[w2b][:, fl * 1024 + Dv * 128: fl * 1024 + (Dv + 1) * 128]
                        B.op("tensor",
                             lambda e, o=pb[bank][:], l=lhs,
                                    r=src[:, fl * TILE:(fl + 1) * TILE],
                                    st=first, sp=last:
                                 e.matmul(o, l, r, start=st, stop=sp),
                             reads=(f"w2s{w2b}", f"hh{fl}", f"hl{fl}"),
                             writes=(f"pb{bank}",),
                             inc="st" if last else None,
                             extra=[w2_grp_ready[G]])
                ysl = ysb[:, Dv * TILE:(Dv + 1) * TILE]
                if G == 0:
                    B.op("scalar", lambda e, o=ysl, i=pb[bank][:]: e.activation(
                             out=o, in_=i, func=AF.Copy),
                         reads=(f"pb{bank}",), writes=("ysb", "ysbA", "ysbB"), inc="ss")
                    yield
                elif G < 3:
                    B.op("vector",
                         lambda e, o=ysl, i=pb[bank][:]:
                             e.tensor_tensor(out=o, in0=o, in1=i, op=OP.add),
                         reads=(f"pb{bank}", "ysb"), writes=("ysb",), inc="sv")
                    yield
                else:
                    B.op("vector",
                         lambda e, o=ysl, i=pb[bank][:], bia=b2s[:, Dv:Dv + 1]:
                             e.scalar_tensor_tensor(out=o, in0=i, scalar=bia, in1=o,
                                                    op0=OP.add, op1=OP.add),
                         reads=(f"pb{bank}", "b2s", "ysb"), writes=("ysb",), inc="sv")
                    B.op("vector",
                         lambda e, o=xsl(buf, Dv), i=ysl:
                             e.scalar_tensor_tensor(out=o, in0=i, scalar=0.5, in1=o,
                                                    op0=OP.mult, op1=OP.add),
                         reads=("ysb", bn), writes=(bn,), inc="sv")
                yield

    def ffn_phase(g, tern_hook=None):
        for _ in ffn_units(g, tern_hook):
            pass

    def qkv_phase(g, t0):
        bn = f"xb{g % 2}"
        buf = xb[g % 2]
        for c in range(NDC):
            k = c % 2
            B.op("vector",
                 lambda e, o=tsub[:], i=xsl(buf, c):
                     e.tensor_tensor(out=o, in0=i, in1=mean[:], op=OP.subtract),
                 reads=(bn, "mean"), writes=("tsub",), inc="sv")
            B.op("vector",
                 lambda e, o=xnf[k][:]:
                     e.tensor_tensor(out=o, in0=tsub[:], in1=rstd[:], op=OP.mult),
                 reads=("tsub", "rstd"), writes=(f"xnf{k}",), inc="sv")
            for bi, (wsb, wn) in enumerate(((swq, "swq"), (swk, "swk"), (swv, "swv"))):
                B.op("tensor",
                     lambda e, o=pb[bi][0:HD, :], l=wsb[:, c * HD:(c + 1) * HD],
                            r=xnf[k][:], st=(c == 0), sp=(c == NDC - 1):
                         e.matmul(o, l, r, start=st, stop=sp),
                     reads=(wn, f"xnf{k}"), writes=(f"pb{bi}",), inc="st")
        B.op("scalar", lambda e: e.activation(out=qtile[:], in_=pb[0][0:HD, :],
                                              func=AF.Identity, bias=bqs[:]),
             reads=("pb0", "bqs"), writes=("qtile",), inc="ss")
        B.op("scalar", lambda e, o=kcache[:, t0:t0 + TILE]: e.activation(
                 out=o, in_=pb[1][0:HD, :], func=AF.Identity, bias=bks[:]),
             reads=("pb1", "bks"), writes=("kcache",), inc="ss")
        B.op("scalar", lambda e: e.activation(out=vtmp[:], in_=pb[2][0:HD, :],
                                              func=AF.Identity, bias=bvs[:]),
             reads=("pb2", "bvs"), writes=("vtmp",), inc="ss")
        for cc in range(TILE // 128):
            blk = t0 // 128 + cc
            B.op("tensor",
                 lambda e, o=pb[6][0:128, 0:HD], i=vtmp[0:HD, cc * 128:(cc + 1) * 128]:
                     e.transpose(o, i, identsb[0:HD, 0:HD]),
                 reads=("vtmp", "identsb"), writes=("pb6",), inc="st")
            B.op("vector",
                 lambda e, o=vtc[:, blk * HD:(blk + 1) * HD], i=pb[6][0:128, 0:HD]:
                     e.tensor_copy(out=o, in_=i),
                 reads=("pb6",), writes=("vtc",), inc="sv")

    def attn_stages(g, t0, pipelined, dump=False):
        bn = f"xb{g % 2}"
        buf = xb[g % 2]
        nsub = TILE // 128

        def srow_of(j):
            if not pipelined:
                return srow, "srow"
            k = j % 3
            if k == 0:
                return srow, "srow"
            if k == 1:
                return ysb[:, 0:S], "ysbA"
            return ysb[:, S:2 * S], "ysbB"

        def stage_scores(j):
            q0 = t0 + 128 * j
            kt = q0 + 128
            nch = (kt + 511) // 512
            sr, srn = srow_of(j)
            par = j % 2
            for ci in range(nch):
                c0 = 512 * ci
                wdt = min(512, kt - c0)
                bank = 4 + ci % 2
                B.op("tensor",
                     lambda e, o=pb[bank][0:128, 0:wdt],
                            l=qtile[0:HD, j * 128:(j + 1) * 128],
                            r=kcache[0:HD, c0:c0 + wdt]:
                         e.matmul(o, l, r, start=True, stop=True),
                     reads=("qtile", "kcache"), writes=(f"pb{bank}",), inc="st")
                if ci == nch - 1:
                    dd = q0 - c0
                    B.op("vector",
                         lambda e, o=sr[:, c0:c0 + wdt], i=pb[bank][0:128, 0:wdt],
                                m=masks[:, 384 - dd:384 - dd + wdt]:
                             e.scalar_tensor_tensor(out=o, in0=i, scalar=0.125, in1=m,
                                                    op0=OP.mult, op1=OP.add),
                         reads=(f"pb{bank}", "masks"), writes=(srn,), inc="sv")
                else:
                    B.op("scalar",
                         lambda e, o=sr[:, c0:c0 + wdt], i=pb[bank][0:128, 0:wdt]:
                             e.activation(out=o, in_=i, func=AF.Copy, scale=0.125),
                         reads=(f"pb{bank}",), writes=(srn,), inc="ss")
            B.op("vector", lambda e, w=kt, o=nmax[par][:], i=sr: e.reduce_max(
                     out=o, in_=i[:, 0:w], axis=AX, negate=True),
                 reads=(srn,), writes=(f"nmax{par}",), inc="sv")
            B.op("scalar", lambda e, w=kt, i=sr, bi=nmax[par][:], ac=rsum[par][:]:
                     e.activation(out=i[:, 0:w], in_=i[:, 0:w], func=AF.Exp,
                                  bias=bi, accum_out=ac),
                 reads=(srn, f"nmax{par}"), writes=(srn, f"rsum{par}"), inc="ss")
            B.op("vector", lambda e, o=rrec[par][:], i=rsum[par][:]:
                     e.reciprocal(out=o, in_=i),
                 reads=(f"rsum{par}",), writes=(f"rrec{par}",), inc="sv")
            B.op("scalar", lambda e, w=kt, i=sr, sc=rrec[par][:]:
                     e.activation(out=i[:, 0:w], in_=i[:, 0:w], func=AF.Identity,
                                  scale=sc),
                 reads=(srn, f"rrec{par}"), writes=(srn,), inc="ss")
            if dump and j == 0:
                dev_dump("srow0", srow[:], ("srow",), (128, S))

        def stage_av(j):
            kt = t0 + 128 * j + 128
            sr, srn = srow_of(j)
            for cc in range(kt // 128):
                pk = cc % 2
                B.op("tensor",
                     lambda e, o=pb[6][0:128, 0:128], i=sr[:, cc * 128:(cc + 1) * 128]:
                         e.transpose(o, i, identsb[:]),
                     reads=(srn, "identsb"), writes=("pb6",), inc="st")
                B.op("scalar",
                     lambda e, o=ptb[pk][:], i=pb[6][0:128, 0:128]:
                         e.activation(out=o, in_=i, func=AF.Copy),
                     reads=("pb6",), writes=(f"ptb{pk}",), inc="ss")
                B.op("tensor",
                     lambda e, o=pb[7][0:HD, 0:128], l=vtc[:, cc * HD:(cc + 1) * HD],
                            r=ptb[pk][:], st=(cc == 0), sp=(cc == kt // 128 - 1):
                         e.matmul(o, l, r, start=st, stop=sp),
                     reads=("vtc", f"ptb{pk}"), writes=("pb7",), inc="st")
            B.op("scalar",
                 lambda e, o=ztile[0:HD, j * 128:(j + 1) * 128], i=pb[7][0:HD, 0:128]:
                     e.activation(out=o, in_=i, func=AF.Copy),
                 reads=("pb7",), writes=("ztile",), inc="ss")

        if pipelined:
            for j in range(min(3, nsub)):
                stage_scores(j)
            for j in range(nsub):
                warm(10)
                stage_av(j)
                if j + 3 < nsub:
                    stage_scores(j + 3)
                yield
        else:
            for j in range(nsub):
                stage_scores(j)
                yield
                stage_av(j)
                yield
        for Dv in range(NDC):
            bank = 4 + Dv % 2
            B.op("tensor",
                 lambda e, o=pb[bank][:], l=swo[0:HD, Dv * 128:(Dv + 1) * 128], r=ztile[0:HD, :]:
                     e.matmul(o, l, r, start=True, stop=True),
                 reads=("swo", "ztile"), writes=(f"pb{bank}",), inc="st")
            B.op("vector",
                 lambda e, o=xsl(buf, Dv), i=pb[bank][:], bia=bos[:, Dv:Dv + 1]:
                     e.scalar_tensor_tensor(out=o, in0=i, scalar=bia, in1=o,
                                            op0=OP.add, op1=OP.add),
                 reads=(f"pb{bank}", "bos", bn), writes=(bn,), inc="sv")

    def attn_phase(g, t0, dump=False):
        for _ in attn_stages(g, t0, pipelined=True, dump=dump):
            pass

    def emit_tap(g, ph):
        if (g, ph) in tap_d:
            buf = xb[g % 2]
            bn = f"xb{g % 2}"
            dma("gpsimd", lambda e, d=tap_d[(g, ph)].ap(), s_=buf[:]:
                    e.dma_start(out=d, in_=s_),
                reads=(bn,), pool="sy")


    stats_phase(0)
    xn_phase(0)

    def emit_thr(wname, col, bank):
        B.op("tensor", lambda e, o=pb[bank][:, 0:1], i=racc2[:, col:col + 1]:
                 e.matmul(o, ones[:], i, start=True, stop=True),
             reads=("ones", "racc2", "racc2b"), writes=(f"pb{bank}",), inc="st")
        B.op("vector",
             lambda e, o=th[wname + "p"][:], i=pb[bank][:, 0:1]:
                 e.tensor_scalar(out=o, in0=i, scalar1=0.5 / NW, scalar2=0.5e-8,
                                 op0=OP.mult, op1=OP.max),
             reads=(f"pb{bank}",), writes=(f"th{wname}p",), inc="sv")
        B.op("vector",
             lambda e, o=th[wname + "n"][:], i=pb[bank][:, 0:1]:
                 e.tensor_scalar(out=o, in0=i, scalar1=-0.5 / NW, scalar2=-0.5e-8,
                                 op0=OP.mult, op1=OP.min),
             reads=(f"pb{bank}",), writes=(f"th{wname}n",), inc="sv")

    scr1_done = []
    scr2_done = []
    pass_a(w1blk, 0)
    B.op("vector", lambda e: e.memset(sqb[0][:], 0.0), writes=("sq0",), inc="sv")
    B.op("vector", lambda e: e.memset(sqb[1][:], 0.0), writes=("sq1",), inc="sv")
    B.op("vector", lambda e: e.reduce_sum(out=racc2[:, 0:1], in_=racc[:, 0:NFC // 2], axis=AX),
         reads=("racc",), writes=("racc2",), inc="sv")
    emit_thr("1", 0, 6)
    for b in range(8):
        tern_block(w1blk, scr1, "1", b, scr1_done)
    pass_a(w2blk, NFC)
    B.op("vector", lambda e: e.memset(sqb[0][:], 0.0), writes=("sq0",), inc="sv")
    B.op("vector", lambda e: e.memset(sqb[1][:], 0.0), writes=("sq1",), inc="sv")
    B.op("vector", lambda e: e.reduce_sum(out=racc2[:, 1:2], in_=racc[:, NFC:NFC + NFC // 2], axis=AX),
         reads=("racc",), writes=("racc2b",), inc="sv")
    emit_thr("2", 1, 7)
    for b in range(8):
        tern_block(w2blk, scr2, "2", b, scr2_done)

    def tile0_tern_hook(G):
        grp = G + 1
        if grp > 3:
            return
        for b in range(8 * grp, 8 * grp + 8):
            tern_block(w1blk, scr1, "1", b, scr1_done)
        for b in range(8 * grp, 8 * grp + 8):
            tern_block(w2blk, scr2, "2", b, scr2_done)
    dev_dump("th1p", th["1p"][:], ("th1p",), (128, 1))
    dev_dump("th2p", th["2p"][:], ("th2p",), (128, 1))

    ntl = dev_tiles if dev_tiles is not None else NTILES

    def emit_ystore(g):
        s_, i_ = divmod(g, NT_SEQ)
        t0_ = i_ * TILE
        buf = xb[g % 2]
        dst = ydst[s_, :, :, t0_:t0_ + TILE]
        dma("gpsimd", lambda e, d=dst, s_=buf[:].rearrange("p (c t) -> p c t", c=NDC):
                e.dma_start(out=d, in_=s_),
            reads=(f"xb{g % 2}",), pool="sy")

    if dev_phases in ("ffn1", "ffn1_noln"):
        for g in range(ntl):
            s, i = divmod(g, NT_SEQ)
            t0 = i * TILE
            if g + 1 < ntl and g != 0:
                emit_xload(g + 1)
            if dev and g == dump_tile:
                dev_dump("xin", xb[g % 2][:], (f"xb{g % 2}",), (128, NDC * TILE))
            if dev_phases == "ffn1_noln":
                buf = xb[g % 2]
                bn = f"xb{g % 2}"
                for c in range(NDC):
                    B.op("scalar",
                         lambda e, o=xsl(xnh, c), i=xsl(buf, c):
                             e.activation(out=o, in_=i, func=AF.Copy),
                         reads=(bn,), writes=(f"xnh{c}",), inc="ss")
                    B.op("vector",
                         lambda e, o=xsl(xnl, c), a=xsl(buf, c), b_=xsl(xnh, c):
                             e.tensor_tensor(out=o, in0=a, in1=b_, op=OP.subtract),
                         reads=(bn, f"xnh{c}"), writes=(f"xnl{c}",), inc="sv")
            else:
                if g > 0:
                    stats_phase(g)
                    xn_phase(g)
            ffn_phase(g, tern_hook=tile0_tern_hook if g == 0 else None)
            if g == 0 and ntl > 1:
                for sem_ in ("sv", "sc0", "sc1", "sc2", "sc3",
                             "sl0", "sl1", "sl2", "sl3"):
                    B.raw_wait("gpsimd", sem_, B.n(sem_))
                emit_xload(1)
            if g == dump_tile:
                dev_dump("hh1", hh[:], tuple(f"hh{c}" for c in range(8)), (128, 8 * TILE), BF16)
                dev_dump("hl1", hl[:], tuple(f"hl{c}" for c in range(8)), (128, 8 * TILE), BF16)
                dev_dump("ysb1", ysb[:], ("ysb",), (128, NDC * TILE))
            emit_ystore(g)
    else:
        _SENT = object()

        class _FG:
            def __init__(self, gen, total):
                self.gen = gen
                self.left = total

            def adv(self, n):
                for _ in range(n):
                    if next(self.gen, _SENT) is _SENT:
                        self.left = 0
                        return
                    self.left -= 1

        ffn_phase(0, tern_hook=tile0_tern_hook)
        emit_tap(0, "ffn1")
        if ntl > 1:
            for sem_ in ("sv", "sc0", "sc1", "sc2", "sc3",
                         "sl0", "sl1", "sl2", "sl3"):
                B.raw_wait("gpsimd", sem_, B.n(sem_))
            emit_xload(1)
        import contextlib

        def scope(name):
            if prof:
                return nc.named_scope(name)
            return contextlib.nullcontext()

        for g in range(ntl):
            s, i = divmod(g, NT_SEQ)
            t0 = i * TILE
            if 1 <= g and g + 1 < ntl:
                emit_xload(g + 1)
            with scope(f"ln2qkv_{g}"):
                stats_phase(g)
                qkv_phase(g, t0)
            if g + 1 < ntl:
                with scope(f"overlap_{g}"):
                    ag = attn_stages(g, t0, pipelined=False)
                    next(ag, None)
                    stats_phase(g + 1)
                    warm(24, banks=(2, 3))
                    xg = xn_units(g + 1)
                    fg = _FG(ffn_units(g + 1), 64)
                    xg_done = False
                    for _ in ag:
                        if not xg_done:
                            for _2 in range(2):
                                if next(xg, _SENT) is _SENT:
                                    xg_done = True
                        elif fg.left > 8:
                            fg.adv(min(10, fg.left - 8))
                    for _ in xg:
                        pass
                    emit_tap(g, "attn")
                with scope(f"ln3_{g}"):
                    stats_phase(g)
                    fg.adv(10 ** 6)
                    emit_tap(g + 1, "ffn1")
                    xn_phase(g)
            else:
                attn_phase(g, t0)
                emit_tap(g, "attn")
                stats_phase(g)
                xn_phase(g)
            with scope(f"ffn2_{g}"):
                ffn_phase(g)
            emit_tap(g, "ffn2")
            emit_ystore(g)

    B.raw_wait("gpsimd", "sy", B.n("sy"))

    with nc.Block() as block:
        def replay(e, eng):
            for it in B.ops[eng]:
                if it[0] == "w":
                    e.wait_ge(sems[it[1]], it[2])
                else:
                    _, fn, inc, by = it
                    ins = fn(e)
                    if inc is not None:
                        ins.then_inc(sems[inc], by)

        @block.sync
        def _(e):
            replay(e, "sync")

        @block.gpsimd
        def _(e):
            replay(e, "gpsimd")

        @block.vector
        def _(e):
            replay(e, "vector")

        @block.scalar
        def _(e):
            replay(e, "scalar")

        @block.tensor
        def _(e):
            replay(e, "tensor")

    return nc


def _host_inputs(L0_attn_wq, L0_attn_bq, L0_attn_wk, L0_attn_bk, L0_attn_wv,
                 L0_attn_bv, L0_attn_wo, L0_attn_bo, L0_ffn_W1, L0_ffn_b1,
                 L0_ffn_W2, L0_ffn_b2):
    f32 = np.float32
    W1 = np.asarray(L0_ffn_W1, dtype=f32)
    W2 = np.asarray(L0_ffn_W2, dtype=f32)
    w1blk = np.ascontiguousarray(
        W1.reshape(NFC, 128, NDC, 128).transpose(0, 3, 2, 1).reshape(NFC, 128, NDC * 128))
    w2blk = np.ascontiguousarray(W2.T.reshape(NFC, 128, D))
    shared = {
        "w1blk": w1blk,
        "w2blk": w2blk,
        "wq": np.ascontiguousarray(np.asarray(L0_attn_wq, dtype=f32)),
        "wk": np.ascontiguousarray(np.asarray(L0_attn_wk, dtype=f32)),
        "wv": np.ascontiguousarray(np.asarray(L0_attn_wv, dtype=f32)),
        "wo": np.ascontiguousarray(np.asarray(L0_attn_wo, dtype=f32)),
        "b1r": np.ascontiguousarray(np.asarray(L0_ffn_b1, dtype=f32).reshape(NFC, 128).T),
        "b2r": np.ascontiguousarray(np.asarray(L0_ffn_b2, dtype=f32).reshape(NDC, 128).T),
        "bor": np.ascontiguousarray(np.asarray(L0_attn_bo, dtype=f32).reshape(NDC, 128).T),
        "bqr": np.ascontiguousarray(np.asarray(L0_attn_bq, dtype=f32).reshape(HD, 1)),
        "bkr": np.ascontiguousarray(np.asarray(L0_attn_bk, dtype=f32).reshape(HD, 1)),
        "bvr": np.ascontiguousarray(np.asarray(L0_attn_bv, dtype=f32).reshape(HD, 1)),
        "identf": np.eye(128, dtype=f32),
    }
    jp = np.arange(512)[None, :]
    rr = np.arange(128)[:, None]
    shared["maskt"] = np.where((jp - 384) > rr, f32(-100000000000.0), f32(0.0)).astype(f32)
    return shared


def kernel(L0_attn_x, L0_attn_wq, L0_attn_bq, L0_attn_wk, L0_attn_bk,
           L0_attn_wv, L0_attn_bv, L0_attn_wo, L0_attn_bo,
           L0_ffn_W1, L0_ffn_b1, L0_ffn_W2, L0_ffn_b2):
    x = np.asarray(L0_attn_x, dtype=np.float32)
    shared = _host_inputs(L0_attn_wq, L0_attn_bq, L0_attn_wk, L0_attn_bk,
                          L0_attn_wv, L0_attn_bv, L0_attn_wo, L0_attn_bo,
                          L0_ffn_W1, L0_ffn_b1, L0_ffn_W2, L0_ffn_b2)
    nc = _build()
    in_maps = []
    for c in range(NCORES):
        m = dict(shared)
        m["xT"] = np.ascontiguousarray(x[NSEQ * c:NSEQ * (c + 1)].transpose(0, 2, 1))
        in_maps.append(m)
    res = run_bass_kernel_spmd(nc, in_maps, core_ids=list(range(NCORES)))
    out = np.empty((NCORES * NSEQ, S, D), np.float32)
    for c in range(NCORES):
        out[NSEQ * c:NSEQ * (c + 1)] = res.results[c]["yT"].transpose(0, 2, 1)
    return out


# version 22 (final)
# speedup vs baseline: 1.0665x; 1.0665x over previous
import numpy as np
import ml_dtypes
import concourse.bass as bass
import concourse.mybir as mybir
from concourse.bass_utils import run_bass_kernel_spmd

F32 = mybir.dt.float32
BF16 = mybir.dt.bfloat16
AX = mybir.AxisListType.X
AF = mybir.ActivationFunctionType
OP = mybir.AluOpType

NCORES = 8
NSEQ = 4
S = 2048
D = 1024
FF = 4096
HD = 64
TILE = 512
NT_SEQ = S // TILE
NTILES = NSEQ * NT_SEQ
NDC = D // 128
NFC = FF // 128
NW = FF * D

ENGS = ("sync", "gpsimd", "vector", "scalar", "tensor")
ESEM = {"tensor": "st", "vector": "sv", "scalar": "ss"}


class Build:

    def __init__(self):
        self.ops = {e: [] for e in ENGS}
        self.cnt = {}
        self.hw = {e: {} for e in ENGS}
        self.lw = {}
        self.lr = {}
        self.pending = {e: [] for e in ENGS}

    def n(self, sem):
        return self.cnt.get(sem, 0)

    def _wait(self, eng, sem, val):
        if val <= self.hw[eng].get(sem, 0):
            return
        assert val <= self.n(sem), f"wait-ahead: {eng} waits {sem}>={val} but only {self.n(sem)} emitted"
        self.hw[eng][sem] = val
        self.ops[eng].append(("w", sem, val))

    def raw_wait(self, eng, sem, val):
        self._wait(eng, sem, val)

    def op(self, eng, fn, reads=(), writes=(), inc=None, by=1, extra=()):
        waits = {}

        def add(ent):
            if ent["eng"] == eng:
                return
            assert ent["val"] is not None, f"pending guard on {ent} needed by {eng}"
            sem, val = ent["sem"], ent["val"]
            waits[sem] = max(waits.get(sem, 0), val)

        for b in reads:
            for e in self.lw.get(b, {}).values():
                add(e)
        for b in writes:
            for e in self.lw.get(b, {}).values():
                add(e)
            for e2 in self.lr.get(b, {}).values():
                add(e2)
        for sem, val in extra:
            waits[sem] = max(waits.get(sem, 0), val)
        for sem in sorted(waits):
            self._wait(eng, sem, waits[sem])

        if inc is not None:
            self.cnt[inc] = self.n(inc) + by
        myc = self.n(inc) if inc is not None else None
        self.ops[eng].append(("i", fn, inc, by))

        if inc is not None:
            ent = {"eng": eng, "sem": inc, "val": myc}
            if inc == ESEM.get(eng):
                for p in self.pending[eng]:
                    p["sem"], p["val"] = inc, myc
                self.pending[eng] = []
        else:
            ent = {"eng": eng, "sem": ESEM[eng], "val": None}
            self.pending[eng].append(ent)
        for b in writes:
            self.lw[b] = {eng: ent}
            self.lr[b] = {}
        for b in reads:
            self.lr.setdefault(b, {})[eng] = ent
        return myc


def _build(taps=None, dev=False, dev_tiles=None, dump_tile=0, dev_phases="all", prof=False):
    taps = taps or []
    nc = bass.Bass("TRN2", target_bir_lowering=False, detect_race_conditions=False)

    xT = nc.dram_tensor("xT", [NSEQ, D, S], F32, kind="ExternalInput")
    w1blk = nc.dram_tensor("w1blk", [NFC, 128, NDC * 128], F32, kind="ExternalInput")
    w2blk = nc.dram_tensor("w2blk", [NFC, 128, D], F32, kind="ExternalInput")
    wq_d = nc.dram_tensor("wq", [D, HD], F32, kind="ExternalInput")
    wk_d = nc.dram_tensor("wk", [D, HD], F32, kind="ExternalInput")
    wv_d = nc.dram_tensor("wv", [D, HD], F32, kind="ExternalInput")
    wo_d = nc.dram_tensor("wo", [HD, D], F32, kind="ExternalInput")
    b1_d = nc.dram_tensor("b1r", [128, NFC], F32, kind="ExternalInput")
    b2_d = nc.dram_tensor("b2r", [128, NDC], F32, kind="ExternalInput")
    bo_d = nc.dram_tensor("bor", [128, NDC], F32, kind="ExternalInput")
    bq_d = nc.dram_tensor("bqr", [HD, 1], F32, kind="ExternalInput")
    bk_d = nc.dram_tensor("bkr", [HD, 1], F32, kind="ExternalInput")
    bv_d = nc.dram_tensor("bvr", [HD, 1], F32, kind="ExternalInput")
    id_d = nc.dram_tensor("identf", [128, 128], F32, kind="ExternalInput")
    mk_d = nc.dram_tensor("maskt", [128, 512], F32, kind="ExternalInput")
    scr_kind = "ExternalOutput" if dev else "Internal"
    scr1 = nc.dram_tensor("scr1", [NFC, 128, NDC * 128], BF16, kind=scr_kind)
    scr2 = nc.dram_tensor("scr2", [NFC, 128, D], BF16, kind=scr_kind)
    yT = nc.dram_tensor("yT", [NSEQ, D, S], F32, kind="ExternalOutput")
    tap_d = {}
    for (g, ph) in taps:
        tap_d[(g, ph)] = nc.dram_tensor(f"tap_{g}_{ph}", [128, NDC * TILE], F32,
                                        kind="ExternalOutput")

    A = nc.alloc_sbuf_tensor
    xb = [A(f"xb{k}", [128, NDC * TILE], F32) for k in range(2)]
    xnh = A("xnh", [128, NDC * TILE], BF16)
    xnl = A("xnl", [128, NDC * TILE], BF16)
    hh = A("hh", [128, 8 * TILE], BF16)
    hl = A("hl", [128, 8 * TILE], BF16)
    ysb = A("ysb", [128, NDC * TILE], F32)
    w1s = [A(f"w1s{k}", [128, 4096], BF16) for k in range(2)]
    w2s = [A(f"w2s{k}", [128, 8192], BF16) for k in range(2)]
    wpro = A("wpro", [128, 2048], F32)
    stg = A("stg", [128, 4096], BF16)
    gbuf = [A(f"g{k}", [128, TILE], F32) for k in range(2)]
    sqb = [A(f"sq{k}", [128, TILE], F32) for k in range(2)]
    mean = A("mean", [128, TILE], F32)
    rstd = A("rstd", [128, TILE], F32)
    t1 = A("t1", [128, TILE], F32)
    tsub = A("tsub", [128, TILE], F32)
    xnf = [A(f"xnf{k}", [128, TILE], F32) for k in range(2)]
    srow = A("srow", [128, S], F32)
    ptb = [A(f"ptb{k}", [128, 128], F32) for k in range(2)]
    qtile = A("qtile", [HD, TILE], F32)
    ztile = A("ztile", [HD, TILE], F32)
    vtmp = A("vtmp", [HD, TILE], F32)
    kcache = A("kcache", [HD, S], F32)
    vtc = A("vtc", [128, (S // 128) * HD], F32)
    swq = A("swq", [128, NDC * HD], F32)
    swk = A("swk", [128, NDC * HD], F32)
    swv = A("swv", [128, NDC * HD], F32)
    swo = A("swo", [HD, D], F32)
    b1s = A("b1s", [128, NFC], F32)
    b2s = A("b2s", [128, NDC], F32)
    bos = A("bos", [128, NDC], F32)
    bqs = A("bqs", [HD, 1], F32)
    bks = A("bks", [HD, 1], F32)
    bvs = A("bvs", [HD, 1], F32)
    identsb = A("identsb", [128, 128], F32)
    masks = A("masks", [128, 512], F32)
    ones = A("ones", [128, 128], F32)
    onesb = A("onesb", [128, 128], BF16)
    junk = A("junk", [128, 512], BF16)
    epss = A("epss", [128, 1], F32)
    racc = A("racc", [128, 64], F32)
    racc2 = A("racc2", [128, 2], F32)
    th = {w: A(f"th_{w}", [128, 1], F32) for w in ("1p", "1n", "2p", "2n")}
    nmax = [A(f"nmax{k}", [128, 1], F32) for k in range(2)]
    rsum = [A(f"rsum{k}", [128, 1], F32) for k in range(2)]
    rrec = [A(f"rrec{k}", [128, 1], F32) for k in range(2)]

    pb = [nc.alloc_psum_tensor(f"pb{i}", [128, 512], F32) for i in range(8)]

    sems = {k: nc.alloc_semaphore(name=k)
            for k in ("sl0", "sl1", "sl2", "sl3", "sc0", "sc1", "sc2", "sc3",
                      "sx", "sy", "st", "sv", "ss")}

    B = Build()
    _pool_idx = {"sl": 0, "sc": 0}

    def dma(eng, fn, reads=(), writes=(), pool="sl", extra=()):
        if pool in _pool_idx:
            names = {"sl": ("sl0", "sl1", "sl2", "sl3"),
                     "sc": ("sc0", "sc1", "sc2", "sc3")}[pool]
            sem = names[_pool_idx[pool] % len(names)]
            _pool_idx[pool] += 1
        else:
            sem = pool
        gate = [(sem, B.n(sem))] + list(extra)
        B.op(eng, fn, reads=reads, writes=writes, inc=sem, by=16, extra=gate)
        return (sem, B.n(sem))

    def xsl(buf, c, lo=0, hi=TILE):
        return buf[:, c * TILE + lo: c * TILE + hi]

    _dumped = set()

    def dev_dump(name, ap, reads, shape, dtype=F32):
        if not dev or name in _dumped:
            return
        _dumped.add(name)
        t = nc.dram_tensor(f"dev_{name}", list(shape), dtype, kind="ExternalOutput")
        dma("gpsimd", lambda e, d=t.ap(), s_=ap: e.dma_start(out=d, in_=s_),
            reads=reads, pool="sy")

    xsrc = xT.ap().rearrange("s (c p) t -> s p c t", p=128)
    ydst = yT.ap().rearrange("s (c p) t -> s p c t", p=128)

    def emit_xload(g):
        s, i = divmod(g, NT_SEQ)
        t0 = i * TILE
        buf = xb[g % 2]
        bn = f"xb{g % 2}"
        src = xsrc[s, :, :, t0:t0 + TILE]
        dst = buf[:].rearrange("p (c t) -> p c t", c=NDC)
        dma("gpsimd", lambda e, d=dst, s_=src: e.dma_start(out=d, in_=s_),
            writes=(bn,), pool="sx")

    emit_xload(0)

    for dst, src, name in (
        (identsb[:], id_d.ap(), "identsb"),
        (masks[:], mk_d.ap(), "masks"),
        (swq[:].rearrange("p (c h) -> p c h", c=NDC),
         wq_d.ap().rearrange("(c p) h -> p c h", p=128), "swq"),
        (swk[:].rearrange("p (c h) -> p c h", c=NDC),
         wk_d.ap().rearrange("(c p) h -> p c h", p=128), "swk"),
        (swv[:].rearrange("p (c h) -> p c h", c=NDC),
         wv_d.ap().rearrange("(c p) h -> p c h", p=128), "swv"),
        (swo[:], wo_d.ap(), "swo"),
        (b1s[:], b1_d.ap(), "b1s"),
        (b2s[:], b2_d.ap(), "b2s"),
        (bos[:], bo_d.ap(), "bos"),
        (bqs[:], bq_d.ap(), "bqs"),
        (bks[:], bk_d.ap(), "bks"),
        (bvs[:], bv_d.ap(), "bvs"),
    ):
        dma("sync", lambda e, d=dst, s_=src: e.dma_start(out=d, in_=s_),
            writes=(name,), pool="sl")

    B.op("vector", lambda e: e.memset(ones[:], 1.0), writes=("ones",), inc="sv")
    B.op("vector", lambda e: e.memset(onesb[:], 1.0), writes=("onesb",), inc="sv")
    B.op("vector", lambda e: e.memset(junk[:], 0.0), writes=("junk",), inc="sv")
    B.op("vector", lambda e: e.memset(epss[:], 1e-5), writes=("epss",), inc="sv")

    _ts = {"i": 0}

    def tslot():
        k = _ts["i"] % 6
        _ts["i"] += 1
        if k < 2:
            return wpro[:, k * 1024:(k + 1) * 1024], f"tslot{k}"
        kk = k - 2
        return xb[1][:, kk * 1024:(kk + 1) * 1024], f"tslot{k}"

    _pairs = [(wpro[:, 0:2048], "tslot0", "tslot1"),
              (xb[1][:, 0:2048], "tslot2", "tslot3"),
              (xb[1][:, 2048:4096], "tslot4", "tslot5")]

    def pass_a(blkdram, col0):
        for k in range(NFC // 2):
            dst, na, nb = _pairs[k % 3]
            src = blkdram.ap()[2 * k:2 * k + 2].rearrange("b p q -> p b q")
            dma("sync", lambda e, d=dst[:].rearrange("p (b q) -> p b q", b=2)
                    if False else dst.rearrange("p (b q) -> p b q", b=2), s_=src:
                    e.dma_start(out=d, in_=s_),
                writes=(na, nb), pool="sl")
            B.op("vector",
                 lambda e, o=racc[:, col0 + k:col0 + k + 1], i=dst:
                     e.reduce_sum(out=o, in_=i, axis=AX, apply_absolute_value=True),
                 reads=(na, nb), writes=("racc",), inc="sv")

    dev_dump("th1p", th["1p"][:], ("th1p",), (128, 1))
    dev_dump("th2p", th["2p"][:], ("th2p",), (128, 1))
    dev_dump("racc2", racc2[:], ("racc2",), (128, 2))

    _tb = {"q": 0}

    def tern_block(blkdram, scr, wname, b, scr_done):
        q = _tb["q"] % 4
        _tb["q"] += 1
        wsrc, hn = tslot()
        dma("sync", lambda e, d=wsrc, s_=blkdram.ap()[b]: e.dma_start(out=d, in_=s_),
            writes=(hn,), pool="sl")
        ge = srow[:, 0:1024]
        le = srow[:, 1024:2048]
        B.op("vector",
             lambda e, o=ge, i=wsrc, t=th[wname + "p"][:]:
                 e.tensor_scalar(out=o, in0=i, scalar1=t, scalar2=None, op0=OP.is_ge),
             reads=(hn, f"th{wname}p"), writes=("srowA",), inc="sv")
        B.op("vector",
             lambda e, o=le, i=wsrc, t=th[wname + "n"][:]:
                 e.tensor_scalar(out=o, in0=i, scalar1=t, scalar2=None, op0=OP.is_le),
             reads=(hn, f"th{wname}n"), writes=("srowB",), inc="sv")
        stq = stg[:, q * 1024:(q + 1) * 1024]
        qn = f"stage{q}"
        B.op("vector",
             lambda e, o=stq, a=ge, b_=le:
                 e.tensor_tensor(out=o, in0=a, in1=b_, op=OP.subtract),
             reads=("srowA", "srowB"), writes=(qn,), inc="sv")
        if q in (1, 3):
            ent = dma("sync",
                      lambda e, d=scr.ap()[b - 1:b + 1],
                             s_=stg[:, (q - 1) * 1024:(q + 1) * 1024]
                                 .rearrange("p (b_ j) -> p b_ j", b_=2):
                          e.dma_start(out=d.rearrange("b_ p j -> p b_ j"), in_=s_),
                      reads=(f"stage{q - 1}", qn), pool="sc")
            scr_done.append(ent)
            scr_done.append(ent)


    def warm(n, banks=(0, 1)):
        for d_ in range(n):
            bank = banks[d_ % 2]
            B.op("tensor",
                 lambda e, o=pb[bank][:]: e.matmul(o, onesb[:], junk[:],
                                                   start=True, stop=True),
                 reads=("onesb", "junk"), writes=(f"pb{bank}",),
                 inc="st" if d_ >= n - 2 else None)

    def stats_phase(g):
        bn = f"xb{g % 2}"
        buf = xb[g % 2]
        for c in range(NDC):
            k = c % 2
            B.op("vector",
                 lambda e, o=sqb[k][:], i=xsl(buf, c):
                     e.tensor_tensor(out=o, in0=i, in1=i, op=OP.mult),
                 reads=(bn,), writes=(f"sq{k}",), inc="sv")
            B.op("tensor",
                 lambda e, i=xsl(buf, c), st=(c == 0), sp=(c == NDC - 1):
                     e.matmul(pb[6][:], ones[:], i, start=st, stop=sp),
                 reads=(bn, "ones"), writes=("pb6",),
                 inc="st" if c == NDC - 1 else None)
            B.op("tensor",
                 lambda e, i=sqb[k][:], st=(c == 0), sp=(c == NDC - 1):
                     e.matmul(pb[7][:], ones[:], i, start=st, stop=sp),
                 reads=(f"sq{k}", "ones"), writes=("pb7",), inc="st")
        B.op("scalar", lambda e: e.activation(out=mean[:], in_=pb[6][:], func=AF.Copy,
                                              scale=1.0 / D),
             reads=("pb6",), writes=("mean",), inc="ss")
        B.op("scalar", lambda e: e.activation(out=t1[:], in_=pb[6][:], func=AF.Square,
                                              scale=1.0 / D),
             reads=("pb6",), writes=("t1",), inc="ss")
        B.op("vector", lambda e: e.scalar_tensor_tensor(out=tsub[:], in0=pb[7][:],
                                                        scalar=1.0 / D, in1=t1[:],
                                                        op0=OP.mult, op1=OP.subtract),
             reads=("pb7", "t1"), writes=("tsub",), inc="sv")
        B.op("scalar", lambda e: e.activation(out=t1[:], in_=tsub[:], func=AF.Sqrt,
                                              bias=epss[:]),
             reads=("tsub", "epss"), writes=("t1",), inc="ss")
        B.op("vector", lambda e: e.reciprocal(out=rstd[:], in_=t1[:]),
             reads=("t1",), writes=("rstd",), inc="sv")

    def xn_units(g):
        bn = f"xb{g % 2}"
        buf = xb[g % 2]
        for c in range(NDC):
            k = c % 2
            B.op("vector",
                 lambda e, o=tsub[:], i=xsl(buf, c):
                     e.tensor_tensor(out=o, in0=i, in1=mean[:], op=OP.subtract),
                 reads=(bn, "mean"), writes=("tsub",), inc="sv")
            B.op("vector",
                 lambda e, o=xnf[k][:]:
                     e.tensor_tensor(out=o, in0=tsub[:], in1=rstd[:], op=OP.mult),
                 reads=("tsub", "rstd"), writes=(f"xnf{k}",), inc="sv")
            B.op("scalar",
                 lambda e, o=xsl(xnh, c), i=xnf[k][:]:
                     e.activation(out=o, in_=i, func=AF.Copy),
                 reads=(f"xnf{k}",), writes=(f"xnh{c}",), inc="ss")
            B.op("vector",
                 lambda e, o=xsl(xnl, c), a=xnf[k][:], b_=xsl(xnh, c):
                     e.tensor_tensor(out=o, in0=a, in1=b_, op=OP.subtract),
                 reads=(f"xnf{k}", f"xnh{c}"), writes=(f"xnl{c}",), inc="sv")
            yield

    def xn_phase(g):
        for _ in xn_units(g):
            pass

    w1_grp_ready = {}
    w2_grp_ready = {}

    def stream_w1(gidx, buf):
        src = scr1.ap()[4 * gidx:4 * gidx + 4].rearrange("b p q -> p b q")
        dst = w1s[buf][:].rearrange("p (b q) -> p b q", b=4)
        w1_grp_ready[gidx] = dma(
            "sync", lambda e, d=dst, s_=src: e.dma_start(out=d, in_=s_),
            writes=(f"w1s{buf}",), pool="sl", extra=[scr1_done[4 * gidx + 3]])

    def stream_w2(gidx, buf):
        src = scr2.ap()[8 * gidx:8 * gidx + 8].rearrange("f p q -> p f q")
        dst = w2s[buf][:].rearrange("p (f q) -> p f q", f=8)
        w2_grp_ready[gidx] = dma(
            "sync", lambda e, d=dst, s_=src: e.dma_start(out=d, in_=s_),
            writes=(f"w2s{buf}",), pool="sl", extra=[scr2_done[8 * gidx + 7]])

    def ffn_units(g, tern_hook=None):
        bn = f"xb{g % 2}"
        buf = xb[g % 2]
        stream_w1(0, 0)
        stream_w1(1, 1)
        stream_w2(0, 0)
        for G in range(4):
            if tern_hook is not None:
                tern_hook(G)
            for bloc in range(8):
                b = 8 * G + bloc
                wbuf = (b // 4) % 2
                bank = b % 2
                for c in range(NDC):
                    for p, src in ((0, xnh), (1, xnl)):
                        first = (c == 0 and p == 0)
                        last = (c == NDC - 1 and p == 1)
                        lhs = w1s[wbuf][:, (b % 4) * 1024 + c * 128:
                                        (b % 4) * 1024 + (c + 1) * 128]
                        B.op("tensor",
                             lambda e, o=pb[bank][:], l=lhs, r=xsl(src, c),
                                    st=first, sp=last:
                                 e.matmul(o, l, r, start=st, stop=sp),
                             reads=(f"w1s{wbuf}", f"xnh{c}", f"xnl{c}"),
                             writes=(f"pb{bank}",),
                             inc="st" if last else None,
                             extra=[w1_grp_ready[b // 4]])
                if b % 4 == 3 and (b // 4) + 2 <= 7:
                    stream_w1((b // 4) + 2, ((b // 4) + 2) % 2)
                gk = b % 2
                B.op("scalar",
                     lambda e, o=gbuf[gk][:], i=pb[bank][:], bia=b1s[:, b:b + 1]:
                         e.activation(out=o, in_=i, func=AF.Gelu, bias=bia),
                     reads=(f"pb{bank}", "b1s"), writes=(f"g{gk}",), inc="ss")
                B.op("scalar",
                     lambda e, o=hh[:, bloc * TILE:(bloc + 1) * TILE], i=gbuf[gk][:]:
                         e.activation(out=o, in_=i, func=AF.Copy),
                     reads=(f"g{gk}",), writes=(f"hh{bloc}",), inc="ss")
                B.op("vector",
                     lambda e, o=hl[:, bloc * TILE:(bloc + 1) * TILE], a=gbuf[gk][:],
                            b_=hh[:, bloc * TILE:(bloc + 1) * TILE]:
                         e.tensor_tensor(out=o, in0=a, in1=b_, op=OP.subtract),
                     reads=(f"g{gk}", f"hh{bloc}"), writes=(f"hl{bloc}",), inc="sv")
                yield
            if G + 1 <= 3:
                stream_w2(G + 1, (G + 1) % 2)
            w2b = G % 2
            for Dv in range(NDC):
                bank = 2 + Dv % 2
                for fl in range(8):
                    for p, src in ((0, hh), (1, hl)):
                        first = (fl == 0 and p == 0)
                        last = (fl == 7 and p == 1)
                        lhs = w2s[w2b][:, fl * 1024 + Dv * 128: fl * 1024 + (Dv + 1) * 128]
                        B.op("tensor",
                             lambda e, o=pb[bank][:], l=lhs,
                                    r=src[:, fl * TILE:(fl + 1) * TILE],
                                    st=first, sp=last:
                                 e.matmul(o, l, r, start=st, stop=sp),
                             reads=(f"w2s{w2b}", f"hh{fl}", f"hl{fl}"),
                             writes=(f"pb{bank}",),
                             inc="st" if last else None,
                             extra=[w2_grp_ready[G]])
                ysl = ysb[:, Dv * TILE:(Dv + 1) * TILE]
                if G == 0:
                    B.op("scalar", lambda e, o=ysl, i=pb[bank][:]: e.activation(
                             out=o, in_=i, func=AF.Copy),
                         reads=(f"pb{bank}",), writes=("ysb", "ysbA", "ysbB"), inc="ss")
                    yield
                elif G < 3:
                    B.op("vector",
                         lambda e, o=ysl, i=pb[bank][:]:
                             e.tensor_tensor(out=o, in0=o, in1=i, op=OP.add),
                         reads=(f"pb{bank}", "ysb"), writes=("ysb",), inc="sv")
                    yield
                else:
                    B.op("vector",
                         lambda e, o=ysl, i=pb[bank][:], bia=b2s[:, Dv:Dv + 1]:
                             e.scalar_tensor_tensor(out=o, in0=i, scalar=bia, in1=o,
                                                    op0=OP.add, op1=OP.add),
                         reads=(f"pb{bank}", "b2s", "ysb"), writes=("ysb",), inc="sv")
                    B.op("vector",
                         lambda e, o=xsl(buf, Dv), i=ysl:
                             e.scalar_tensor_tensor(out=o, in0=i, scalar=0.5, in1=o,
                                                    op0=OP.mult, op1=OP.add),
                         reads=("ysb", bn), writes=(bn,), inc="sv")
                yield

    def ffn_phase(g, tern_hook=None):
        for _ in ffn_units(g, tern_hook):
            pass

    def qkv_phase(g, t0):
        bn = f"xb{g % 2}"
        buf = xb[g % 2]
        for c in range(NDC):
            k = c % 2
            B.op("vector",
                 lambda e, o=tsub[:], i=xsl(buf, c):
                     e.tensor_tensor(out=o, in0=i, in1=mean[:], op=OP.subtract),
                 reads=(bn, "mean"), writes=("tsub",), inc="sv")
            B.op("vector",
                 lambda e, o=xnf[k][:]:
                     e.tensor_tensor(out=o, in0=tsub[:], in1=rstd[:], op=OP.mult),
                 reads=("tsub", "rstd"), writes=(f"xnf{k}",), inc="sv")
            for bi, (wsb, wn) in enumerate(((swq, "swq"), (swk, "swk"), (swv, "swv"))):
                B.op("tensor",
                     lambda e, o=pb[bi][0:HD, :], l=wsb[:, c * HD:(c + 1) * HD],
                            r=xnf[k][:], st=(c == 0), sp=(c == NDC - 1):
                         e.matmul(o, l, r, start=st, stop=sp),
                     reads=(wn, f"xnf{k}"), writes=(f"pb{bi}",), inc="st")
        B.op("scalar", lambda e: e.activation(out=qtile[:], in_=pb[0][0:HD, :],
                                              func=AF.Identity, bias=bqs[:]),
             reads=("pb0", "bqs"), writes=("qtile",), inc="ss")
        B.op("scalar", lambda e, o=kcache[:, t0:t0 + TILE]: e.activation(
                 out=o, in_=pb[1][0:HD, :], func=AF.Identity, bias=bks[:]),
             reads=("pb1", "bks"), writes=("kcache",), inc="ss")
        B.op("scalar", lambda e: e.activation(out=vtmp[:], in_=pb[2][0:HD, :],
                                              func=AF.Identity, bias=bvs[:]),
             reads=("pb2", "bvs"), writes=("vtmp",), inc="ss")
        for cc in range(TILE // 128):
            blk = t0 // 128 + cc
            B.op("tensor",
                 lambda e, o=pb[6][0:128, 0:HD], i=vtmp[0:HD, cc * 128:(cc + 1) * 128]:
                     e.transpose(o, i, identsb[0:HD, 0:HD]),
                 reads=("vtmp", "identsb"), writes=("pb6",), inc="st")
            B.op("vector",
                 lambda e, o=vtc[:, blk * HD:(blk + 1) * HD], i=pb[6][0:128, 0:HD]:
                     e.tensor_copy(out=o, in_=i),
                 reads=("pb6",), writes=("vtc",), inc="sv")

    def attn_stages(g, t0, pipelined, dump=False):
        bn = f"xb{g % 2}"
        buf = xb[g % 2]
        nsub = TILE // 128

        def srow_of(j):
            if not pipelined:
                return srow, "srow"
            k = j % 3
            if k == 0:
                return srow, "srow"
            if k == 1:
                return ysb[:, 0:S], "ysbA"
            return ysb[:, S:2 * S], "ysbB"

        def stage_scores(j):
            q0 = t0 + 128 * j
            kt = q0 + 128
            nch = (kt + 511) // 512
            sr, srn = srow_of(j)
            par = j % 2
            for ci in range(nch):
                c0 = 512 * ci
                wdt = min(512, kt - c0)
                bank = 4 + ci % 2
                B.op("tensor",
                     lambda e, o=pb[bank][0:128, 0:wdt],
                            l=qtile[0:HD, j * 128:(j + 1) * 128],
                            r=kcache[0:HD, c0:c0 + wdt]:
                         e.matmul(o, l, r, start=True, stop=True),
                     reads=("qtile", "kcache"), writes=(f"pb{bank}",), inc="st")
                if ci == nch - 1:
                    dd = q0 - c0
                    B.op("vector",
                         lambda e, o=sr[:, c0:c0 + wdt], i=pb[bank][0:128, 0:wdt],
                                m=masks[:, 384 - dd:384 - dd + wdt]:
                             e.scalar_tensor_tensor(out=o, in0=i, scalar=0.125, in1=m,
                                                    op0=OP.mult, op1=OP.add),
                         reads=(f"pb{bank}", "masks"), writes=(srn,), inc="sv")
                else:
                    B.op("scalar",
                         lambda e, o=sr[:, c0:c0 + wdt], i=pb[bank][0:128, 0:wdt]:
                             e.activation(out=o, in_=i, func=AF.Copy, scale=0.125),
                         reads=(f"pb{bank}",), writes=(srn,), inc="ss")
            B.op("vector", lambda e, w=kt, o=nmax[par][:], i=sr: e.reduce_max(
                     out=o, in_=i[:, 0:w], axis=AX, negate=True),
                 reads=(srn,), writes=(f"nmax{par}",), inc="sv")
            B.op("scalar", lambda e, w=kt, i=sr, bi=nmax[par][:], ac=rsum[par][:]:
                     e.activation(out=i[:, 0:w], in_=i[:, 0:w], func=AF.Exp,
                                  bias=bi, accum_out=ac),
                 reads=(srn, f"nmax{par}"), writes=(srn, f"rsum{par}"), inc="ss")
            B.op("vector", lambda e, o=rrec[par][:], i=rsum[par][:]:
                     e.reciprocal(out=o, in_=i),
                 reads=(f"rsum{par}",), writes=(f"rrec{par}",), inc="sv")
            B.op("scalar", lambda e, w=kt, i=sr, sc=rrec[par][:]:
                     e.activation(out=i[:, 0:w], in_=i[:, 0:w], func=AF.Identity,
                                  scale=sc),
                 reads=(srn, f"rrec{par}"), writes=(srn,), inc="ss")
            if dump and j == 0:
                dev_dump("srow0", srow[:], ("srow",), (128, S))

        def stage_av(j):
            kt = t0 + 128 * j + 128
            sr, srn = srow_of(j)
            for cc in range(kt // 128):
                pk = cc % 2
                B.op("tensor",
                     lambda e, o=pb[6][0:128, 0:128], i=sr[:, cc * 128:(cc + 1) * 128]:
                         e.transpose(o, i, identsb[:]),
                     reads=(srn, "identsb"), writes=("pb6",), inc="st")
                B.op("scalar",
                     lambda e, o=ptb[pk][:], i=pb[6][0:128, 0:128]:
                         e.activation(out=o, in_=i, func=AF.Copy),
                     reads=("pb6",), writes=(f"ptb{pk}",), inc="ss")
                B.op("tensor",
                     lambda e, o=pb[7][0:HD, 0:128], l=vtc[:, cc * HD:(cc + 1) * HD],
                            r=ptb[pk][:], st=(cc == 0), sp=(cc == kt // 128 - 1):
                         e.matmul(o, l, r, start=st, stop=sp),
                     reads=("vtc", f"ptb{pk}"), writes=("pb7",), inc="st")
            B.op("scalar",
                 lambda e, o=ztile[0:HD, j * 128:(j + 1) * 128], i=pb[7][0:HD, 0:128]:
                     e.activation(out=o, in_=i, func=AF.Copy),
                 reads=("pb7",), writes=("ztile",), inc="ss")

        if pipelined:
            for j in range(min(3, nsub)):
                stage_scores(j)
            for j in range(nsub):
                warm(10)
                stage_av(j)
                if j + 3 < nsub:
                    stage_scores(j + 3)
                yield
        else:
            for j in range(nsub):
                stage_scores(j)
                yield
                stage_av(j)
                yield
        for Dv in range(NDC):
            bank = 4 + Dv % 2
            B.op("tensor",
                 lambda e, o=pb[bank][:], l=swo[0:HD, Dv * 128:(Dv + 1) * 128], r=ztile[0:HD, :]:
                     e.matmul(o, l, r, start=True, stop=True),
                 reads=("swo", "ztile"), writes=(f"pb{bank}",), inc="st")
            B.op("vector",
                 lambda e, o=xsl(buf, Dv), i=pb[bank][:], bia=bos[:, Dv:Dv + 1]:
                     e.scalar_tensor_tensor(out=o, in0=i, scalar=bia, in1=o,
                                            op0=OP.add, op1=OP.add),
                 reads=(f"pb{bank}", "bos", bn), writes=(bn,), inc="sv")

    def attn_phase(g, t0, dump=False):
        for _ in attn_stages(g, t0, pipelined=True, dump=dump):
            pass

    def emit_tap(g, ph):
        if (g, ph) in tap_d:
            buf = xb[g % 2]
            bn = f"xb{g % 2}"
            dma("gpsimd", lambda e, d=tap_d[(g, ph)].ap(), s_=buf[:]:
                    e.dma_start(out=d, in_=s_),
                reads=(bn,), pool="sy")


    stats_phase(0)
    xn_phase(0)

    def emit_thr(wname, col, bank):
        B.op("tensor", lambda e, o=pb[bank][:, 0:1], i=racc2[:, col:col + 1]:
                 e.matmul(o, ones[:], i, start=True, stop=True),
             reads=("ones", "racc2", "racc2b"), writes=(f"pb{bank}",), inc="st")
        B.op("vector",
             lambda e, o=th[wname + "p"][:], i=pb[bank][:, 0:1]:
                 e.tensor_scalar(out=o, in0=i, scalar1=0.5 / NW, scalar2=0.5e-8,
                                 op0=OP.mult, op1=OP.max),
             reads=(f"pb{bank}",), writes=(f"th{wname}p",), inc="sv")
        B.op("vector",
             lambda e, o=th[wname + "n"][:], i=pb[bank][:, 0:1]:
                 e.tensor_scalar(out=o, in0=i, scalar1=-0.5 / NW, scalar2=-0.5e-8,
                                 op0=OP.mult, op1=OP.min),
             reads=(f"pb{bank}",), writes=(f"th{wname}n",), inc="sv")

    scr1_done = []
    scr2_done = []
    pass_a(w1blk, 0)
    B.op("vector", lambda e: e.memset(sqb[0][:], 0.0), writes=("sq0",), inc="sv")
    B.op("vector", lambda e: e.memset(sqb[1][:], 0.0), writes=("sq1",), inc="sv")
    B.op("vector", lambda e: e.reduce_sum(out=racc2[:, 0:1], in_=racc[:, 0:NFC // 2], axis=AX),
         reads=("racc",), writes=("racc2",), inc="sv")
    emit_thr("1", 0, 6)
    for b in range(8):
        tern_block(w1blk, scr1, "1", b, scr1_done)
    pass_a(w2blk, NFC)
    B.op("vector", lambda e: e.memset(sqb[0][:], 0.0), writes=("sq0",), inc="sv")
    B.op("vector", lambda e: e.memset(sqb[1][:], 0.0), writes=("sq1",), inc="sv")
    B.op("vector", lambda e: e.reduce_sum(out=racc2[:, 1:2], in_=racc[:, NFC:NFC + NFC // 2], axis=AX),
         reads=("racc",), writes=("racc2b",), inc="sv")
    emit_thr("2", 1, 7)
    for b in range(8):
        tern_block(w2blk, scr2, "2", b, scr2_done)

    def tile0_tern_hook(G):
        grp = G + 1
        if grp > 3:
            return
        for b in range(8 * grp, 8 * grp + 8):
            tern_block(w1blk, scr1, "1", b, scr1_done)
        for b in range(8 * grp, 8 * grp + 8):
            tern_block(w2blk, scr2, "2", b, scr2_done)
    dev_dump("th1p", th["1p"][:], ("th1p",), (128, 1))
    dev_dump("th2p", th["2p"][:], ("th2p",), (128, 1))

    ntl = dev_tiles if dev_tiles is not None else NTILES

    def emit_ystore(g):
        s_, i_ = divmod(g, NT_SEQ)
        t0_ = i_ * TILE
        buf = xb[g % 2]
        dst = ydst[s_, :, :, t0_:t0_ + TILE]
        dma("gpsimd", lambda e, d=dst, s_=buf[:].rearrange("p (c t) -> p c t", c=NDC):
                e.dma_start(out=d, in_=s_),
            reads=(f"xb{g % 2}",), pool="sy")

    if dev_phases in ("ffn1", "ffn1_noln"):
        for g in range(ntl):
            s, i = divmod(g, NT_SEQ)
            t0 = i * TILE
            if g + 1 < ntl and g != 0:
                emit_xload(g + 1)
            if dev and g == dump_tile:
                dev_dump("xin", xb[g % 2][:], (f"xb{g % 2}",), (128, NDC * TILE))
            if dev_phases == "ffn1_noln":
                buf = xb[g % 2]
                bn = f"xb{g % 2}"
                for c in range(NDC):
                    B.op("scalar",
                         lambda e, o=xsl(xnh, c), i=xsl(buf, c):
                             e.activation(out=o, in_=i, func=AF.Copy),
                         reads=(bn,), writes=(f"xnh{c}",), inc="ss")
                    B.op("vector",
                         lambda e, o=xsl(xnl, c), a=xsl(buf, c), b_=xsl(xnh, c):
                             e.tensor_tensor(out=o, in0=a, in1=b_, op=OP.subtract),
                         reads=(bn, f"xnh{c}"), writes=(f"xnl{c}",), inc="sv")
            else:
                if g > 0:
                    stats_phase(g)
                    xn_phase(g)
            ffn_phase(g, tern_hook=tile0_tern_hook if g == 0 else None)
            if g == 0 and ntl > 1:
                for sem_ in ("sv", "sc0", "sc1", "sc2", "sc3",
                             "sl0", "sl1", "sl2", "sl3"):
                    B.raw_wait("gpsimd", sem_, B.n(sem_))
                emit_xload(1)
            if g == dump_tile:
                dev_dump("hh1", hh[:], tuple(f"hh{c}" for c in range(8)), (128, 8 * TILE), BF16)
                dev_dump("hl1", hl[:], tuple(f"hl{c}" for c in range(8)), (128, 8 * TILE), BF16)
                dev_dump("ysb1", ysb[:], ("ysb",), (128, NDC * TILE))
            emit_ystore(g)
    else:
        _SENT = object()

        class _FG:
            def __init__(self, gen, total):
                self.gen = gen
                self.left = total

            def adv(self, n):
                for _ in range(n):
                    if next(self.gen, _SENT) is _SENT:
                        self.left = 0
                        return
                    self.left -= 1

        ffn_phase(0, tern_hook=tile0_tern_hook)
        emit_tap(0, "ffn1")
        if ntl > 1:
            for sem_ in ("sv", "sc0", "sc1", "sc2", "sc3",
                         "sl0", "sl1", "sl2", "sl3"):
                B.raw_wait("gpsimd", sem_, B.n(sem_))
            emit_xload(1)
        import contextlib

        def scope(name):
            if prof:
                return nc.named_scope(name)
            return contextlib.nullcontext()

        for g in range(ntl):
            s, i = divmod(g, NT_SEQ)
            t0 = i * TILE
            if 1 <= g and g + 1 < ntl:
                emit_xload(g + 1)
            with scope(f"ln2qkv_{g}"):
                stats_phase(g)
                qkv_phase(g, t0)
            if g + 1 < ntl:
                with scope(f"overlap_{g}"):
                    ag = attn_stages(g, t0, pipelined=False)
                    next(ag, None)
                    stats_phase(g + 1)
                    warm(24, banks=(2, 3))
                    xg = xn_units(g + 1)
                    fg = _FG(ffn_units(g + 1), 64)
                    xg_done = False
                    for _ in ag:
                        if not xg_done:
                            for _2 in range(2):
                                if next(xg, _SENT) is _SENT:
                                    xg_done = True
                        elif fg.left > 8:
                            fg.adv(min(10, fg.left - 8))
                    for _ in xg:
                        pass
                    emit_tap(g, "attn")
                with scope(f"ln3_{g}"):
                    stats_phase(g)
                    fg.adv(10 ** 6)
                    emit_tap(g + 1, "ffn1")
                    xn_phase(g)
            else:
                attn_phase(g, t0)
                emit_tap(g, "attn")
                stats_phase(g)
                xn_phase(g)
            with scope(f"ffn2_{g}"):
                ffn_phase(g)
            emit_tap(g, "ffn2")
            emit_ystore(g)

    B.raw_wait("gpsimd", "sy", B.n("sy"))

    with nc.Block() as block:
        def replay(e, eng):
            for it in B.ops[eng]:
                if it[0] == "w":
                    e.wait_ge(sems[it[1]], it[2])
                else:
                    _, fn, inc, by = it
                    ins = fn(e)
                    if inc is not None:
                        ins.then_inc(sems[inc], by)

        @block.sync
        def _(e):
            replay(e, "sync")

        @block.gpsimd
        def _(e):
            replay(e, "gpsimd")

        @block.vector
        def _(e):
            replay(e, "vector")

        @block.scalar
        def _(e):
            replay(e, "scalar")

        @block.tensor
        def _(e):
            replay(e, "tensor")

    return nc


def _host_inputs(L0_attn_wq, L0_attn_bq, L0_attn_wk, L0_attn_bk, L0_attn_wv,
                 L0_attn_bv, L0_attn_wo, L0_attn_bo, L0_ffn_W1, L0_ffn_b1,
                 L0_ffn_W2, L0_ffn_b2):
    f32 = np.float32
    W1 = np.asarray(L0_ffn_W1, dtype=f32)
    W2 = np.asarray(L0_ffn_W2, dtype=f32)
    w1blk = np.ascontiguousarray(
        W1.reshape(NFC, 128, NDC, 128).transpose(0, 3, 2, 1).reshape(NFC, 128, NDC * 128))
    w2blk = np.ascontiguousarray(W2.T.reshape(NFC, 128, D))
    shared = {
        "w1blk": w1blk,
        "w2blk": w2blk,
        "wq": np.ascontiguousarray(np.asarray(L0_attn_wq, dtype=f32)),
        "wk": np.ascontiguousarray(np.asarray(L0_attn_wk, dtype=f32)),
        "wv": np.ascontiguousarray(np.asarray(L0_attn_wv, dtype=f32)),
        "wo": np.ascontiguousarray(np.asarray(L0_attn_wo, dtype=f32)),
        "b1r": np.ascontiguousarray(np.asarray(L0_ffn_b1, dtype=f32).reshape(NFC, 128).T),
        "b2r": np.ascontiguousarray(np.asarray(L0_ffn_b2, dtype=f32).reshape(NDC, 128).T),
        "bor": np.ascontiguousarray(np.asarray(L0_attn_bo, dtype=f32).reshape(NDC, 128).T),
        "bqr": np.ascontiguousarray(np.asarray(L0_attn_bq, dtype=f32).reshape(HD, 1)),
        "bkr": np.ascontiguousarray(np.asarray(L0_attn_bk, dtype=f32).reshape(HD, 1)),
        "bvr": np.ascontiguousarray(np.asarray(L0_attn_bv, dtype=f32).reshape(HD, 1)),
        "identf": np.eye(128, dtype=f32),
    }
    jp = np.arange(512)[None, :]
    rr = np.arange(128)[:, None]
    shared["maskt"] = np.where((jp - 384) > rr, f32(-100000000000.0), f32(0.0)).astype(f32)
    return shared


def kernel(L0_attn_x, L0_attn_wq, L0_attn_bq, L0_attn_wk, L0_attn_bk,
           L0_attn_wv, L0_attn_bv, L0_attn_wo, L0_attn_bo,
           L0_ffn_W1, L0_ffn_b1, L0_ffn_W2, L0_ffn_b2):
    x = np.asarray(L0_attn_x, dtype=np.float32)
    shared = _host_inputs(L0_attn_wq, L0_attn_bq, L0_attn_wk, L0_attn_bk,
                          L0_attn_wv, L0_attn_bv, L0_attn_wo, L0_attn_bo,
                          L0_ffn_W1, L0_ffn_b1, L0_ffn_W2, L0_ffn_b2)
    nc = _build()
    in_maps = []
    for c in range(NCORES):
        m = dict(shared)
        m["xT"] = np.ascontiguousarray(x[NSEQ * c:NSEQ * (c + 1)].transpose(0, 2, 1))
        in_maps.append(m)
    res = run_bass_kernel_spmd(nc, in_maps, core_ids=list(range(NCORES)))
    out = np.empty((NCORES * NSEQ, S, D), np.float32)
    for c in range(NCORES):
        out[NSEQ * c:NSEQ * (c + 1)] = res.results[c]["yT"].transpose(0, 2, 1)
    return out
